# version 01
# baseline (speedup 1.0000x reference)
import contextlib
import numpy as np
import ml_dtypes
import concourse.bass as bass
import concourse.mybir as mybir
from concourse.bass_utils import run_bass_kernel_spmd

F32 = mybir.dt.float32
BF16 = mybir.dt.bfloat16
AF = mybir.ActivationFunctionType
ALU = mybir.AluOpType
AX = mybir.AxisListType
BF = ml_dtypes.bfloat16

NCORES = 8
TOK = 2048
D = 1024
H = 16
DK = 64
CH = 128
NCHUNK = TOK // CH
NGROUP = 4
I_FFN = 2816
GROW = I_FFN // NCORES
DROW = D // NCORES

M1 = float(2 ** 22)
M2 = float(2 ** 23)
M3 = float(2 ** 24)

CONSTS = {"eps": 1e-5, "m2": -2.0, "m4": -4.0, "p2": 2.0}
CONST_IDX = {k: i for i, k in enumerate(CONSTS)}

ENGINES = ("sync", "vector", "scalar", "tensor", "gpsimd")

I_BLOCKS = [(0, 512), (512, 512), (1024, 512), (1536, 512), (2048, 512), (2560, 256)]
D_PANELS = [list(range(0, 8)), list(range(8, 16)), list(range(16, 22))]


def yz_slot(pbank, h):
    if h < 7:
        return pbank[0][:, 65 * h:65 * h + 65]
    if h < 14:
        return pbank[1][:, 65 * (h - 7):65 * (h - 7) + 65]
    return pbank[2][:, 65 * (h - 14):65 * (h - 14) + 65]


class Prog:
    def __init__(self):
        self.items = {e: [] for e in ENGINES}

    def on(self, e):
        def deco(f):
            self.items[e].append(f)
            return f
        return deco


class Sem:
    def __init__(self, ec, nc, name):
        self.h = ec(nc.semaphore(name))
        self.n = 0

    def bump(self, k):
        self.n += k
        return self.n


class Build:
    def __init__(self, debug=(), upto="full"):
        self.debug = debug
        self.upto = upto
        self.nc = bass.Bass("TRN2", target_bir_lowering=False)
        self.prog = Prog()

    def dram_t(self, name, shape, dtype):
        kind = "ExternalOutput" if name in self.debug else "Internal"
        return self.nc.dram_tensor(name, shape, dtype, kind=kind).ap()

    def dma(self, queue, sem, out, in_, waits=(), **kw):
        post = sem.bump(16)

        @self.prog.on(queue)
        def _(eng, out=out, in_=in_, kw=kw, waits=tuple(waits)):
            for s, v in waits:
                eng.wait_ge(s.h, v)
            eng.dma_start(out=out, in_=in_, **kw).then_inc(sem.h, 16)
        return post

    def consts(self, name):
        return self.consts_sb[:, CONST_IDX[name]:CONST_IDX[name] + 1]

    def run_block(self):
        block = self.nc.Block()
        with block:
            for e in ENGINES:
                if not self.prog.items[e]:
                    continue
                section = getattr(block, e)

                @section
                def _(eng, e=e):
                    for f in self.prog.items[e]:
                        f(eng)
        self.prog = Prog()

    def declare_global(self):
        nc, ec = self.nc, self.ec
        self.x_in = nc.dram_tensor("x", [TOK, D], F32, kind="ExternalInput").ap()
        self.consts_in = nc.dram_tensor("consts", [128, len(CONSTS)], F32, kind="ExternalInput").ap()
        self.wqT_in = nc.dram_tensor("wqT", [D, D], F32, kind="ExternalInput").ap()
        self.wkT_in = nc.dram_tensor("wkT", [D, D], F32, kind="ExternalInput").ap()
        self.wvT_in = nc.dram_tensor("wvT", [D, D], F32, kind="ExternalInput").ap()
        self.woT_in = nc.dram_tensor("woT", [D, D], F32, kind="ExternalInput").ap()
        self.mask_in = nc.dram_tensor("mask", [128, 128], BF16, kind="ExternalInput").ap()
        self.prefw_in = nc.dram_tensor("prefw", [128, NCORES], F32, kind="ExternalInput").ap()
        self.wg_in = nc.dram_tensor("wg_sh", [GROW, D], F32, kind="ExternalInput").ap()
        self.wu_in = nc.dram_tensor("wu_sh", [GROW, D], F32, kind="ExternalInput").ap()
        self.wd_in = nc.dram_tensor("wd_sh", [DROW, I_FFN], F32, kind="ExternalInput").ap()
        self.out_dram = nc.dram_tensor("out", [TOK, D], F32, kind="ExternalOutput").ap()

        self.xnb = self.dram_t("xnb", [TOK, D], BF16)
        self.xqb = self.dram_t("xqb", [TOK, D], BF16)
        self.pqb = self.dram_t("pqb", [D, TOK], BF16)
        self.pkb = self.dram_t("pkb", [D, TOK], BF16)
        self.vb = self.dram_t("vb", [TOK, H * 65], BF16)
        self.yb = self.dram_t("yb", [TOK, D], BF16)
        self.hqb = self.dram_t("hqb", [TOK, I_FFN], BF16)
        self.st_in = nc.dram_tensor("st_in", [128, 520], F32, kind="Internal").ap()
        self.st_out = nc.dram_tensor("st_out", [NCORES * 128, 520], F32, kind="Internal").ap()
        if "st_dbg" in self.debug:
            self.st_dbg = nc.dram_tensor("st_dbg", [NCORES * 128, 520], F32, kind="ExternalOutput").ap()
        self.gq_in = nc.dram_tensor("gq_in", [GROW, D], BF16, kind="Internal").ap()
        self.gq_out = nc.dram_tensor("gq_out", [I_FFN, D], BF16, kind="Internal").ap()
        self.uq_in = nc.dram_tensor("uq_in", [GROW, D], BF16, kind="Internal").ap()
        self.uq_out = nc.dram_tensor("uq_out", [I_FFN, D], BF16, kind="Internal").ap()
        self.dq_in = nc.dram_tensor("dq_in", [DROW, I_FFN], BF16, kind="Internal").ap()
        self.dq_out = nc.dram_tensor("dq_out", [D, I_FFN], BF16, kind="Internal").ap()
        if "dqin_dbg" in self.debug:
            self.dqin_dbg = nc.dram_tensor("dqin_dbg", [DROW, I_FFN], BF16, kind="ExternalOutput").ap()
        for nm, t in (("gq_dbg", [I_FFN, D]), ("uq_dbg", [I_FFN, D]), ("dq_dbg", [D, I_FFN])):
            if nm in self.debug:
                setattr(self, nm, nc.dram_tensor(nm, t, BF16, kind="ExternalOutput").ap())

        sb = lambda name, shape, dt=F32: ec(nc.sbuf_tensor(name, shape, dt))
        self.consts_sb = sb("consts_sb", [128, len(CONSTS)])
        self.mask_sb = sb("mask_sb", [128, 128], BF16)
        self.acc = sb("acc", [128, NCHUNK * 1024])
        self.q_a = sb("q_a", [128, 1024])
        self.q_b = sb("q_b", [128, 1024])
        self.q_c = sb("q_c", [128, 1024])
        self.q_d = sb("q_d", [128, 1024])
        self.bm_t = sb("bm_t", [128, 64])
        self.sc_t = sb("sc_t", [128, 64])
        self.rc_t = sb("rc_t", [128, 64])
        self.xnb_t = sb("xnb_t", [128, 1024], BF16)
        self.xq_t = sb("xq_t", [128, 1024], BF16)
        self.tot_sb = sb("tot_sb", [128, 520])
        self.state_sb = sb("state_sb", [128, 520])
        self.state_bf = sb("state_bf", [128, 520], BF16)
        self.pkt = sb("pkt", [128, 2, 1024], BF16)
        self.vch = sb("vch", [128, 2, 1040], BF16)
        self.pref_sb = sb("pref_sb", [128, NCORES])

        self.pbank = [ec(nc.psum_tensor(f"pb{i}", [128, 512], F32)) for i in range(8)]

        for name in ["s_xload", "s_a2d", "s_d2a", "s_ln", "s_xnst", "s_xqst",
                     "s_cst", "s_out", "s_wat", "s_fmld", "s_mm",
                     "s_ev1", "s_ev2", "s_ev3", "s_pst", "s_vmm", "s_vev",
                     "s_vst", "s_kvld", "s_kvmm", "s_kvadd", "s_stst", "s_cc",
                     "s_stld", "s_pref", "s_maskld", "s_wsh", "s_wqst",
                     "s_p2ld", "s_stm", "s_msk", "s_yzm", "s_zdn", "s_yev",
                     "s_yst", "s_st2", "s_kv2m", "s_wo2", "s_yfmld", "s_womm",
                     "s_woev", "s_xr2", "s_xqld", "s_wgld", "s_gmm", "s_sil",
                     "s_hmul", "s_hqst", "s_hqld", "s_wdld", "s_dmm", "s_dev"]:
            setattr(self, name, Sem(self.ec, nc, name))

    def emit_quant(self, x_ap, out_ap, xwaits=(), dwaits=()):
        prog = self.prog
        P, F = x_ap.shape
        NB = F // 16
        I32 = mybir.dt.int32
        q_a, q_b, q_c, q_d = self.q_a, self.q_b, self.q_c, self.q_d
        bm, sc, rc = self.bm_t, self.sc_t, self.rc_t

        def s3(ap):
            return ap.rearrange("p (a b) -> p a b", b=16)

        @prog.on("vector")
        def _(vector):
            for s, v in tuple(xwaits) + tuple(dwaits):
                vector.wait_ge(s.h, v)
            vector.tensor_reduce(bm[:P, :NB], s3(x_ap), AX.X, ALU.max,
                                 apply_absolute_value=True)
            vector.drain()
            vector.tensor_scalar(sc[:P, :NB], bm[:P, :NB],
                                 float(np.float32(1.0 / 6.0)), 1e-12, ALU.mult, ALU.max)
            vector.drain()
            vector.reciprocal(rc[:P, :NB], sc[:P, :NB])
            vector.drain()
            rc_b = rc[:P, :NB].unsqueeze(2).broadcast_to([P, NB, 16])
            sc_bI = sc[:P, :NB].unsqueeze(2).broadcast_to([P, NB, 16]).bitcast(I32)
            vector.tensor_scalar(q_a[:P, :F].bitcast(I32), x_ap.bitcast(I32),
                                 0x7FFFFFFF, None, ALU.bitwise_and, ALU.bypass)
            vector.tensor_scalar(q_b[:P, :F].bitcast(I32), x_ap.bitcast(I32),
                                 -2147483648, None, ALU.bitwise_and, ALU.bypass)
            vector.tensor_tensor(s3(q_b[:P, :F].bitcast(I32)), s3(q_b[:P, :F].bitcast(I32)),
                                 sc_bI, ALU.bitwise_or)
            vector.tensor_tensor(s3(q_c[:P, :F]), s3(q_a[:P, :F]), rc_b, ALU.mult)
            vector.tensor_scalar(q_a[:P, :F], q_c[:P, :F], -1.0, 4.0, ALU.mult, ALU.add)
            vector.tensor_scalar(q_a[:P, :F], q_a[:P, :F], 0.0, 2.0, ALU.max, ALU.min)
            vector.tensor_scalar(q_a[:P, :F], q_a[:P, :F], -1.0, float(2.0 + M2), ALU.mult, ALU.add)
            vector.tensor_scalar(q_d[:P, :F], q_c[:P, :F], 4.0, 0.0, ALU.subtract, ALU.max)
            vector.tensor_scalar(q_d[:P, :F], q_d[:P, :F], M3, None, ALU.add, ALU.bypass)
            vector.tensor_scalar(q_c[:P, :F], q_c[:P, :F], M1, M1 + 2.0, ALU.add, ALU.min)
            vector.scalar_tensor_tensor(q_a[:P, :F], q_a[:P, :F], M2, q_c[:P, :F],
                                        ALU.subtract, ALU.add)
            vector.scalar_tensor_tensor(q_d[:P, :F], q_d[:P, :F], M3 + M1, q_a[:P, :F],
                                        ALU.subtract, ALU.add)
            vector.tensor_tensor(out_ap, q_d[:P, :F], q_b[:P, :F], ALU.mult)

    def emit_preamble(self):
        self.dma("sync", self.s_cst, self.consts_sb[:], self.consts_in)
        self.dma("sync", self.s_maskld, self.mask_sb[:], self.mask_in)
        for w_in, w_sb in [(self.wqT_in, self.wqTb), (self.wkT_in, self.wkTb),
                           (self.wvT_in, self.wvTb)]:
            self.dma("gpsimd", self.s_wat, w_sb[:],
                     w_in.rearrange("(a p) m -> p a m", p=128))

        @self.prog.on("scalar")
        def _(scalar):
            from concourse.hw_specs import get_activation_tables
            for idx, funcs in enumerate(get_activation_tables(self.nc.m.arch).values()):
                fs = set(funcs)
                if AF.Ln in fs and AF.Exp in fs and AF.Copy in fs:
                    scalar.add_instruction(mybir.InstLoadActFuncSet(
                        name=self.nc.get_next_instruction_name(),
                        act_func_set_id=idx, ins=[], outs=[]))
                    break

        self.w_ones = self.s_d2a.bump(1)

        @self.prog.on("vector")
        def _(vector):
            vector.memset(self.tot_sb[:], 0.0)
            for b in range(2):
                ones_ap = self.vstage[:, b, :].rearrange("p (h c) -> p h c", c=65)[:, :, 64:65]
                vector.memset(ones_ap, 1.0)
            vector.drain().then_inc(self.s_d2a.h, 1)

    def emit_p0_chunk(self, i):
        prog = self.prog
        buf = i % 2
        if i >= 2:
            @prog.on("sync")
            def _(sync, w=16 * (i - 1)):
                sync.wait_ge(self.s_xqst.h, w)
        w_x = self.dma("sync", self.s_xload, self.xc[:, buf, :],
                       self.x_in[128 * i:128 * (i + 1), :])

        w_ln1 = self.s_ln.bump(1)

        @prog.on("vector")
        def _(vector, buf=buf, w_x=w_x, first=(i == 0)):
            vector.wait_ge(self.s_xload.h, w_x)
            if first:
                vector.wait_ge(self.s_cst.h, 16)
                vector.wait_ge(self.s_d2a.h, self.w_ones)
            vector.bn_stats(self.bs_t[:, 0:6], self.xc[:, buf, 0:512])
            vector.bn_stats(self.bs_t[:, 6:12], self.xc[:, buf, 512:1024])
            vector.drain()
            vector.bn_aggr(self.mv_t[:], self.bs_t[:])
            vector.drain().then_inc(self.s_ln.h, 1)

        w_ln2 = self.s_ln.bump(1)

        @prog.on("scalar")
        def _(scalar, w=w_ln1, first=(i == 0)):
            if first:
                scalar.wait_ge(self.s_cst.h, 16)
            scalar.wait_ge(self.s_ln.h, w)
            scalar.activation(self.rstd_t[:], self.mv_t[:, 1:2], AF.Ln, bias=self.consts("eps"))
            scalar.activation(self.rstd_t[:], self.rstd_t[:], AF.Exp, scale=-0.5)
            scalar.drain().then_inc(self.s_ln.h, 1)

        w_xn = self.s_d2a.bump(1)

        @prog.on("vector")
        def _(vector, buf=buf, w=w_ln2, i=i):
            vector.wait_ge(self.s_ln.h, w)
            if i >= 1:
                vector.wait_ge(self.s_xnst.h, 16 * i)
            vector.tensor_scalar(self.xn32[:], self.xc[:, buf, :], self.mv_t[:, 0:1],
                                 self.rstd_t[:], ALU.subtract, ALU.mult)
            vector.tensor_copy(self.xnb_t[:], self.xn32[:])
            vector.drain().then_inc(self.s_d2a.h, 1)

        self.dma("sync", self.s_xnst, self.xnb[128 * i:128 * (i + 1), :], self.xnb_t[:],
                 waits=[(self.s_d2a, w_xn)])

        dwaits = [(self.s_xqst, 16 * i)] if i >= 1 else []
        self.emit_quant(self.xn32[:], self.xq_t[:], xwaits=[(self.s_d2a, w_xn)],
                        dwaits=dwaits)

        w_q = self.s_d2a.bump(1)

        @prog.on("vector")
        def _(vector):
            vector.drain().then_inc(self.s_d2a.h, 1)

        self.dma("sync", self.s_xqst, self.xqb[128 * i:128 * (i + 1), :], self.xq_t[:],
                 waits=[(self.s_d2a, w_q)])

    def emit_wquant_tile(self, k, src_ap, dst_ap, P, F):
        waits = [(self.s_wqst, 16 * k)] if k >= 1 else []
        w_l = self.dma("sync", self.s_wsh, self.wsh32[:P, :F], src_ap, waits=waits)
        self.emit_quant(self.wsh32[:P, :F], self.wshq[:P, :F],
                        xwaits=[(self.s_wsh, w_l), (self.s_d2a, self.s_d2a.n)])
        w_q = self.s_d2a.bump(1)

        @self.prog.on("vector")
        def _(vector):
            vector.drain().then_inc(self.s_d2a.h, 1)

        self.dma("sync", self.s_wqst, dst_ap, self.wshq[:P, :F],
                 waits=[(self.s_d2a, w_q)])

    def emit_w_collective(self, idx, cc_in, cc_out, wq_count):
        if idx == 2 and "dqin_dbg" in self.debug:
            @self.prog.on("gpsimd")
            def _(gpsimd, w=16 * wq_count):
                gpsimd.wait_ge(self.s_wqst.h, w)
                gpsimd.dma_start(out=self.dqin_dbg, in_=self.dq_in).then_inc(self.s_stld.h, 16)
            self.s_stld.bump(16)

        @self.prog.on("gpsimd")
        def _(gpsimd):
            gpsimd.wait_ge(self.s_wqst.h, 16 * wq_count)
            gpsimd.collective_compute(
                "AllGather", ALU.bypass, replica_groups=[list(range(NCORES))],
                ins=[cc_in.opt()], outs=[cc_out.opt()],
            ).then_inc(self.s_cc.h)
        w_cc = self.s_cc.bump(1)
        dbg_name = ("gq_dbg", "uq_dbg", "dq_dbg")[idx]
        if dbg_name in self.debug:
            dbg_ap = getattr(self, dbg_name)
            nrows = dbg_ap.shape[0]

            @self.prog.on("gpsimd")
            def _(gpsimd, w=w_cc, dbg_ap=dbg_ap, cc_out=cc_out, nrows=nrows):
                gpsimd.wait_ge(self.s_cc.h, w)
                for r0 in range(0, nrows, 1024):
                    r1 = min(r0 + 1024, nrows)
                    gpsimd.dma_start(out=dbg_ap[r0:r1, :],
                                     in_=cc_out[r0:r1, :]).then_inc(self.s_stld.h, 16)
            self.s_stld.bump(16 * ((nrows + 1023) // 1024))

    def emit_p1_group(self, g):
        prog = self.prog
        gbuf = g % 2
        for a in range(8):
            waits = [(self.s_xnst, 16 * (4 * g + 4))] if a == 0 else []
            if g >= 2 and a == 0:
                waits.append((self.s_mm, self.group_mm_after[g - 2]))
                waits.append((self.s_vmm, 8 * (g - 1)))
            self.dma("sync", self.s_fmld, self.xnfm[:, gbuf, a, :],
                     self.xnb[512 * g:512 * (g + 1), 128 * a:128 * (a + 1)],
                     waits=waits, transpose=True)
        w_fm = self.s_fmld.n

        for proj, w_sb, dst in [(0, self.wqTb, self.pqb), (1, self.wkTb, self.pkb)]:
            for o in range(8):
                t = self.ptile_idx
                self.ptile_idx += 1
                w_mm = self.s_mm.bump(1)

                @prog.on("tensor")
                def _(tensor, w_sb=w_sb, o=o, t=t, gbuf=gbuf, w_fm=w_fm,
                      first=(t == 0)):
                    if first:
                        tensor.wait_ge(self.s_wat.h, 48)
                    tensor.wait_ge(self.s_fmld.h, w_fm)
                    if t >= 4:
                        tensor.wait_ge(self.s_ev3.h, t - 3)
                    for a in range(8):
                        inst = tensor.matmul(self.pbank[t % 4][:],
                                             w_sb[:, a, 128 * o:128 * (o + 1)],
                                             self.xnfm[:, gbuf, a, :],
                                             start=(a == 0), stop=(a == 7))
                    inst.then_inc(self.s_mm.h, 1)

                w_e1 = self.s_ev1.bump(1)

                @prog.on("vector")
                def _(vector, t=t, w=w_mm):
                    vector.wait_ge(self.s_mm.h, w)
                    if t >= 1:
                        vector.wait_ge(self.s_ev2.h, t)
                    vector.tensor_scalar(self.e_a[:], self.pbank[t % 4][:], 0.0, None,
                                         ALU.min, ALU.bypass)
                    vector.drain().then_inc(self.s_ev1.h, 1)

                w_e2 = self.s_ev2.bump(1)

                @prog.on("scalar")
                def _(scalar, w=w_e1):
                    scalar.wait_ge(self.s_ev1.h, w)
                    scalar.activation(self.e_b[:], self.e_a[:], AF.Exp)
                    scalar.drain().then_inc(self.s_ev2.h, 1)

                w_e3 = self.s_ev3.bump(1)

                @prog.on("vector")
                def _(vector, t=t, w=w_e2):
                    vector.wait_ge(self.s_ev2.h, w)
                    if t >= 2:
                        vector.wait_ge(self.s_pst.h, 16 * (t - 1))
                    vector.scalar_tensor_tensor(self.phist[:, t % 2, :],
                                                self.pbank[t % 4][:], 0.0,
                                                self.e_b[:], ALU.max, ALU.add)
                    vector.drain().then_inc(self.s_ev3.h, 1)

                self.dma("scalar", self.s_pst,
                         dst[128 * o:128 * (o + 1), 512 * g:512 * (g + 1)],
                         self.phist[:, t % 2, :], waits=[(self.s_ev3, w_e3)])

        for ci in range(4):
            i = 4 * g + ci
            for half in range(2):
                u = self.vtile_idx
                self.vtile_idx += 1
                w_mm = self.s_vmm.bump(1)

                @prog.on("tensor")
                def _(tensor, ci=ci, half=half, u=u, gbuf=gbuf, w_fm=w_fm):
                    tensor.wait_ge(self.s_fmld.h, w_fm)
                    if u >= 2:
                        tensor.wait_ge(self.s_vev.h, u - 1)
                    for a in range(8):
                        inst = tensor.matmul(self.pbank[4 + u % 2][:],
                                             self.xnfm[:, gbuf, a, 128 * ci:128 * (ci + 1)],
                                             self.wvTb[:, a, 512 * half:512 * (half + 1)],
                                             start=(a == 0), stop=(a == 7))
                    inst.then_inc(self.s_vmm.h, 1)

                w_ev = self.s_vev.bump(1)

                @prog.on("scalar")
                def _(scalar, i=i, half=half, u=u, w=w_mm):
                    scalar.wait_ge(self.s_vmm.h, w)
                    if half == 0 and i >= 2:
                        scalar.wait_ge(self.s_vst.h, 16 * (i - 1))
                    dst3 = self.vstage[:, i % 2, :].rearrange("p (h c) -> p h c", c=65)[:, 8 * half:8 * (half + 1), 0:64]
                    scalar.activation(dst3, self.pbank[4 + u % 2][:].rearrange("p (h c) -> p h c", c=64),
                                      AF.Copy)
                    scalar.drain().then_inc(self.s_vev.h, 1)
            self.dma("scalar", self.s_vst, self.vb[128 * i:128 * (i + 1), :],
                     self.vstage[:, i % 2, :], waits=[(self.s_vev, self.s_vev.n)])
        self.group_mm_after[g] = self.s_mm.n

    def emit_kv1_chunk(self, i):
        prog = self.prog
        buf = i % 2
        g = i // 4
        waits_pk = [(self.s_pst, 16 * 16 * (g + 1))]
        if i >= 2:
            waits_pk.append((self.s_kvmm, i - 1))
        w_pk = self.dma("sync", self.s_kvld, self.pkt[:, buf, :],
                        self.pkb[:, 128 * i:128 * (i + 1)], transpose=True,
                        waits=waits_pk)
        w_v = self.dma("sync", self.s_kvld, self.vch[:, buf, :],
                       self.vb[128 * i:128 * (i + 1), :],
                       waits=[(self.s_vst, 16 * (i + 1))])
        w_mm = self.s_kvmm.bump(1)

        @prog.on("tensor")
        def _(tensor, i=i, buf=buf, w=max(w_pk, w_v)):
            tensor.wait_ge(self.s_kvld.h, w)
            if i >= 1:
                tensor.wait_ge(self.s_kvadd.h, i)
            for h in range(H):
                p = h // 2
                base = 64 * (h % 2)
                bank = self.pbank[6] if p < 7 else self.pbank[7]
                col = 65 * p if p < 7 else 0
                inst = tensor.matmul(bank[base:base + 64, col:col + 65],
                                     self.pkt[:, buf, 64 * h:64 * (h + 1)],
                                     self.vch[:, buf, 65 * h:65 * (h + 1)],
                                     start=(h in (0, 1, 14, 15)), stop=(h in (12, 13, 15)))
            inst.then_inc(self.s_kvmm.h, 1)

        @prog.on("vector")
        def _(vector, w=w_mm):
            vector.wait_ge(self.s_kvmm.h, w)
            vector.tensor_tensor(self.tot_sb[:, 0:455], self.tot_sb[:, 0:455],
                                 self.pbank[6][:, 0:455], ALU.add)
            vector.tensor_tensor(self.tot_sb[:, 455:520], self.tot_sb[:, 455:520],
                                 self.pbank[7][:, 0:65], ALU.add)
            vector.drain().then_inc(self.s_kvadd.h, 1)
        self.s_kvadd.bump(1)

    def emit_state_exchange(self):
        prog = self.prog
        self.dma("sync", self.s_stst, self.st_in, self.tot_sb[:],
                 waits=[(self.s_kvadd, self.s_kvadd.n)])
        self.s_cc.bump(1)

        @prog.on("gpsimd")
        def _(gpsimd):
            gpsimd.wait_ge(self.s_stst.h, self.s_stst.n)
            gpsimd.collective_compute(
                "AllGather", ALU.bypass, replica_groups=[list(range(NCORES))],
                ins=[self.st_in.opt()], outs=[self.st_out.opt()],
            ).then_inc(self.s_cc.h)

        w_cc_state = self.s_cc.n
        w_ld = self.s_stld.bump(16 * (NCORES + 1))

        @prog.on("gpsimd")
        def _(gpsimd, w=w_cc_state):
            gpsimd.wait_ge(self.s_cc.h, w)
            gpsimd.dma_start(out=self.pref_sb[:], in_=self.prefw_in).then_inc(self.s_stld.h, 16)
            for r in range(NCORES):
                gpsimd.dma_start(out=self.stld_sb[:, r, :],
                                 in_=self.st_out[128 * r:128 * (r + 1), :]).then_inc(self.s_stld.h, 16)

        if "st_dbg" in self.debug:
            @prog.on("gpsimd")
            def _(gpsimd, w=w_cc_state):
                gpsimd.wait_ge(self.s_cc.h, w)
                for r in range(NCORES):
                    gpsimd.dma_start(out=self.st_dbg[128 * r:128 * (r + 1), :],
                                     in_=self.st_out[128 * r:128 * (r + 1), :]).then_inc(self.s_stld.h, 16)
            self.s_stld.bump(16 * NCORES)

        self.w_pref = self.s_pref.bump(1)

        @prog.on("vector")
        def _(vector, w=w_ld):
            vector.wait_ge(self.s_stld.h, w)
            vector.memset(self.state_sb[:], 0.0)
            vector.drain()
            for r in range(NCORES):
                vector.scalar_tensor_tensor(self.state_sb[:], self.stld_sb[:, r, :],
                                            self.pref_sb[:, r:r + 1], self.state_sb[:],
                                            ALU.mult, ALU.add)
            vector.tensor_copy(self.state_bf[:], self.state_sb[:])
            vector.drain().then_inc(self.s_pref.h, 1)

    def emit_block_a(self):
        nc = self.nc
        with contextlib.ExitStack() as stack:
            ec = stack.enter_context
            sb = lambda name, shape, dt=F32: ec(nc.sbuf_tensor(name, shape, dt))
            self.wqTb = sb("wqTb", [128, 8, 1024], BF16)
            self.wkTb = sb("wkTb", [128, 8, 1024], BF16)
            self.wvTb = sb("wvTb", [128, 8, 1024], BF16)
            self.xnfm = sb("xnfm", [128, 2, 8, 512], BF16)
            self.xc = sb("xc", [128, 2, 1024])
            self.xn32 = sb("xn32", [128, 1024])
            self.bs_t = sb("bs_t", [128, 12])
            self.mv_t = sb("mv_t", [128, 2])
            self.rstd_t = sb("rstd_t", [128, 1])
            self.e_a = sb("e_a", [128, 512])
            self.e_b = sb("e_b", [128, 512])
            self.phist = sb("phist", [128, 2, 512], BF16)
            self.vstage = sb("vstage", [128, 2, 1040], BF16)
            self.stld_sb = sb("stld_sb", [128, NCORES, 520])
            self.wsh32 = sb("wsh32", [128, 1024])
            self.wshq = sb("wshq", [128, 1024], BF16)

            self.ptile_idx = 0
            self.vtile_idx = 0
            self.group_mm_after = {}

            self.emit_preamble()
            wq_tiles = []
            for r0 in (0, 128, 256):
                P = min(128, GROW - r0)
                wq_tiles.append((self.wg_in[r0:r0 + P, :], self.gq_in[r0:r0 + P, :], P, 1024))
            for r0 in (0, 128, 256):
                P = min(128, GROW - r0)
                wq_tiles.append((self.wu_in[r0:r0 + P, :], self.uq_in[r0:r0 + P, :], P, 1024))
            for c0, fc in ((0, 1024), (1024, 1024), (2048, 768)):
                wq_tiles.append((self.wd_in[:, c0:c0 + fc], self.dq_in[:, c0:c0 + fc], 128, fc))

            k = 0
            p0_range = [] if self.upto == "wq" else range(NCHUNK)
            for i in list(p0_range) + list(range(16, 26)):
                if i < NCHUNK:
                    self.emit_p0_chunk(i)
                if (i >= 4 or self.upto == "wq") and k < len(wq_tiles):
                    src, dstp, P, F = wq_tiles[k]
                    self.emit_wquant_tile(k, src, dstp, P, F)
                    k += 1
                    if k == 3:
                        self.emit_w_collective(0, self.gq_in, self.gq_out, 3)
                    elif k == 6:
                        self.emit_w_collective(1, self.uq_in, self.uq_out, 6)
                    elif k == 9:
                        self.emit_w_collective(2, self.dq_in, self.dq_out, 9)
            assert k == len(wq_tiles)

            if self.upto != "wq":
                for g in range(NGROUP):
                    self.emit_p1_group(g)
                for i in range(NCHUNK):
                    self.emit_kv1_chunk(i)
                self.emit_state_exchange()

            if self.upto == "a":
                self.emit_placeholder_out()
            if self.upto == "wq":
                self.emit_placeholder_out_wq()
            self.run_block()

    def emit_phase2_chunk(self, n):
        prog = self.prog
        buf = n % 2
        waits0 = [(self.s_kv2m, n - 1)] if n >= 2 else []
        w_l1 = self.dma("sync", self.s_p2ld, self.pqt[:, buf, :, :],
                        self.pqb.rearrange("(o p) t -> p o t", p=128)[:, :, 128 * n:128 * (n + 1)],
                        waits=waits0)
        w_l2 = self.dma("sync", self.s_p2ld, self.pkf[:, buf, :, :],
                        self.pkb.rearrange("(o p) t -> p o t", p=128)[:, :, 128 * n:128 * (n + 1)])
        w_l3 = self.dma("sync", self.s_p2ld, self.pkt[:, buf, :],
                        self.pkb[:, 128 * n:128 * (n + 1)], transpose=True)
        w_l4 = self.dma("sync", self.s_p2ld, self.vch[:, buf, :],
                        self.vb[128 * n:128 * (n + 1), :])
        w_ld = w_l4

        def pq_slice(h):
            base = 64 * (h % 2)
            return self.pqt[base:base + 64, buf, h // 2, :]

        def pk_slice(h):
            base = 64 * (h % 2)
            return self.pkf[base:base + 64, buf, h // 2, :]

        def st_slice(h):
            base = 64 * (h % 2)
            return self.state_bf[base:base + 64, 65 * (h // 2):65 * (h // 2) + 65]

        @prog.on("tensor")
        def _(tensor, n=n, buf=buf, w_ld=w_ld):
            tensor.wait_ge(self.s_p2ld.h, w_ld)
            if n == 0:
                tensor.wait_ge(self.s_pref.h, self.w_pref)
                tensor.wait_ge(self.s_ev3.h, 64)
                tensor.wait_ge(self.s_vev.h, 32)
                tensor.wait_ge(self.s_kvadd.h, 16)
            else:
                tensor.wait_ge(self.s_st2.h, n)
                tensor.wait_ge(self.s_yev.h, 16 * n)

            def emit_yz(h):
                tensor.wait_ge(self.s_msk.h, 16 * n + h + 1)
                tensor.matmul(yz_slot(self.pbank, h), self.smt[:, h % 2, :],
                              self.vch[:, buf, 65 * h:65 * h + 65],
                              start=True, stop=False)
                inst = tensor.matmul(yz_slot(self.pbank, h), pq_slice(h), st_slice(h),
                                     start=False, stop=True)
                inst.then_inc(self.s_yzm.h, 1)

            for h in range(H):
                if h >= 2:
                    tensor.wait_ge(self.s_msk.h, 16 * n + h - 1)
                tensor.matmul(self.pbank[4 + h % 2][:, 0:128], pk_slice(h), pq_slice(h),
                              start=True, stop=True).then_inc(self.s_stm.h, 1)
                if h >= 1:
                    emit_yz(h - 1)
            emit_yz(H - 1)
            for h in range(H):
                p = h // 2
                base = 64 * (h % 2)
                bank = self.pbank[6] if p < 7 else self.pbank[7]
                col = 65 * p if p < 7 else 0
                inst = tensor.matmul(bank[base:base + 64, col:col + 65],
                                     self.pkt[:, buf, 64 * h:64 * (h + 1)],
                                     self.vch[:, buf, 65 * h:65 * (h + 1)],
                                     start=(h in (0, 1, 14, 15)), stop=(h in (12, 13, 15)))
            inst.then_inc(self.s_kv2m.h, 1)
        self.s_stm.bump(H)
        self.s_yzm.bump(H)
        self.s_kv2m.bump(1)

        w_zdn = self.s_zdn.bump(1)
        w_st2 = self.s_st2.bump(1)

        @prog.on("vector")
        def _(vector, n=n):
            for h in range(H):
                vector.wait_ge(self.s_stm.h, 16 * n + h + 1)
                vector.tensor_tensor(self.smt[:, h % 2, :], self.pbank[4 + h % 2][:, 0:128],
                                     self.mask_sb[:], ALU.mult)
                vector.drain().then_inc(self.s_msk.h, 1)
            vector.wait_ge(self.s_yzm.h, 16 * (n + 1))
            for bi, nh, off in ((0, 7, 0), (1, 7, 7), (2, 2, 14)):
                zview = self.pbank[bi][:, 0:65 * nh].rearrange("p (h c) -> p h c", c=65)[:, :, 64:65]
                vector.tensor_scalar(self.dn_t[:, off:off + nh].unsqueeze(2), zview,
                                     1e-6, None, ALU.max, ALU.bypass)
            vector.drain()
            vector.reciprocal(self.rc2_t[:], self.dn_t[:])
            vector.drain().then_inc(self.s_zdn.h, 1)
            vector.wait_ge(self.s_kv2m.h, n + 1)
            vector.tensor_tensor(self.state_sb[:, 0:455], self.state_sb[:, 0:455],
                                 self.pbank[6][:, 0:455], ALU.add)
            vector.tensor_tensor(self.state_sb[:, 455:520], self.state_sb[:, 455:520],
                                 self.pbank[7][:, 0:65], ALU.add)
            vector.tensor_copy(self.state_bf[:], self.state_sb[:])
            vector.drain().then_inc(self.s_st2.h, 1)
        self.s_msk.bump(H)

        @prog.on("scalar")
        def _(scalar, n=n, w=w_zdn):
            scalar.wait_ge(self.s_zdn.h, w)
            if n >= 2:
                scalar.wait_ge(self.s_yst.h, 16 * (n - 1))
            for h in range(H):
                scalar.activation(self.ytile[:, n % 2, 64 * h:64 * (h + 1)],
                                  yz_slot(self.pbank, h)[:, 0:64],
                                  AF.Copy, scale=self.rc2_t[:, h:h + 1])
            scalar.drain().then_inc(self.s_yev.h, 16)
        self.s_yev.bump(16)

        self.dma("scalar", self.s_yst, self.yb[128 * n:128 * (n + 1), :],
                 self.ytile[:, n % 2, :], waits=[(self.s_yev, 16 * (n + 1))])

    def emit_wo(self):
        prog = self.prog
        self.dma("gpsimd", self.s_wo2, self.woTb[:],
                 self.woT_in.rearrange("(a p) m -> p a m", p=128))
        for o in range(8):
            self.dma("sync", self.s_yfmld, self.yfm[:, o, :],
                     self.yb[:, 128 * o:128 * (o + 1)], transpose=True,
                     waits=[(self.s_yst, 16 * NCHUNK)] if o == 0 else [])
        w_yfm = self.s_yfmld.n

        for i in range(NCHUNK):
            w_x = self.dma("sync", self.s_xr2, self.xc2[:, i % 2, :],
                           self.x_in[128 * i:128 * (i + 1), :],
                           waits=[(self.s_woev, 2 * (i - 1))] if i >= 2 else [])
            for half in range(2):
                u = 2 * i + half
                w_mm = self.s_womm.bump(1)

                @prog.on("tensor")
                def _(tensor, i=i, half=half, u=u, w_yfm=w_yfm, first=(i == 0 and half == 0)):
                    if first:
                        tensor.wait_ge(self.s_wo2.h, 16)
                        tensor.wait_ge(self.s_yfmld.h, w_yfm)
                        tensor.wait_ge(self.s_st2.h, NCHUNK)
                        tensor.wait_ge(self.s_zdn.h, NCHUNK)
                        tensor.wait_ge(self.s_yev.h, 16 * NCHUNK)
                    if u >= 4:
                        tensor.wait_ge(self.s_woev.h, u - 3)
                    for o in range(8):
                        inst = tensor.matmul(self.pbank[u % 4][:],
                                             self.yfm[:, o, 128 * i:128 * (i + 1)],
                                             self.woTb[:, o, 512 * half:512 * (half + 1)],
                                             start=(o == 0), stop=(o == 7))
                    inst.then_inc(self.s_womm.h, 1)

                w_ev = self.s_woev.bump(1)

                @prog.on("vector")
                def _(vector, i=i, half=half, u=u, w=w_mm, w_x=w_x):
                    vector.wait_ge(self.s_womm.h, w)
                    vector.wait_ge(self.s_xr2.h, w_x)
                    vector.tensor_tensor(self.acc[:, 1024 * i + 512 * half:1024 * i + 512 * (half + 1)],
                                         self.pbank[u % 4][:],
                                         self.xc2[:, i % 2, 512 * half:512 * (half + 1)],
                                         ALU.add)
                    vector.drain().then_inc(self.s_woev.h, 1)

    def emit_block_b(self):
        nc = self.nc
        with contextlib.ExitStack() as stack:
            ec = stack.enter_context
            sb = lambda name, shape, dt=F32: ec(nc.sbuf_tensor(name, shape, dt))
            self.pqt = sb("pqt", [128, 2, 8, 128], BF16)
            self.pkf = sb("pkf", [128, 2, 8, 128], BF16)
            self.smt = sb("smt", [128, 2, 128], BF16)
            self.ytile = sb("ytile", [128, 2, 1024], BF16)
            self.dn_t = sb("dn_t", [128, 16])
            self.rc2_t = sb("rc2_t", [128, 16])
            self.woTb = sb("woTb", [128, 8, 1024], BF16)
            self.yfm = sb("yfm", [128, 8, 2048], BF16)
            self.xc2 = sb("xc2", [128, 2, 1024])

            for n in range(NCHUNK):
                self.emit_phase2_chunk(n)
            self.emit_wo()
            if self.upto == "b":
                for i in range(NCHUNK):
                    self.dma("sync", self.s_out, self.out_dram[128 * i:128 * (i + 1), :],
                             self.acc[:, 1024 * i:1024 * (i + 1)],
                             waits=[(self.s_woev, 2 * (i + 1))])

                @self.prog.on("gpsimd")
                def _(gpsimd):
                    gpsimd.wait_ge(self.s_out.h, self.s_out.n)
            self.run_block()

    def emit_block_c(self):
        nc = self.nc
        prog = self.prog
        with contextlib.ExitStack() as stack:
            ec = stack.enter_context
            sb = lambda name, shape, dt=F32: ec(nc.sbuf_tensor(name, shape, dt))
            self.xqfm = sb("xqfm", [128, 8, 2048], BF16)
            self.wgld = sb("wgld", [128, 2, 2, 8, 512], BF16)
            self.sgate = sb("sgate", [128, 512])
            self.h32 = sb("h32", [128, 2, 512])
            self.hqt2 = sb("hqt2", [128, 2, 512], BF16)

            @prog.on("scalar")
            def _(scalar):
                scalar.preload_act_table(AF.Silu)
                scalar.activation(self.sgate[:, 0:8], self.sgate[:, 0:8], AF.Silu)
                scalar.drain()

            for a in range(8):
                self.dma("sync", self.s_xqld, self.xqfm[:, a, :],
                         self.xqb[:, 128 * a:128 * (a + 1)], transpose=True,
                         waits=[(self.s_xqst, 16 * NCHUNK)] if a == 0 else [])
            w_xq = self.s_xqld.n

            qa_after = {}
            unit = 0
            for b, (i0, fb) in enumerate(I_BLOCKS):
                wbuf = b % 2
                for mi, gout in ((0, self.gq_out), (1, self.uq_out)):
                    for a in range(8):
                        waits = []
                        if b == 0 and mi == 0 and a == 0:
                            waits.append((self.s_cc, 2))
                        if b >= 2 and mi == 0 and a == 0:
                            waits.append((self.s_gmm, 16 * (b - 1)))
                        self.dma("sync", self.s_wgld, self.wgld[:, mi, wbuf, a, 0:fb],
                                 gout[i0:i0 + fb, 128 * a:128 * (a + 1)],
                                 waits=waits, transpose=True)
                w_wg = self.s_wgld.n

                for c in range(NCHUNK):
                    cbuf = unit % 2
                    gbank = self.pbank[cbuf]
                    ubank = self.pbank[2 + cbuf]
                    w_mm = self.s_gmm.bump(1)

                    @prog.on("tensor")
                    def _(tensor, c=c, wbuf=wbuf, fb=fb, cbuf=cbuf, unit=unit,
                          w_wg=w_wg, w_xq=w_xq, first=(unit == 0)):
                        if first:
                            tensor.wait_ge(self.s_xqld.h, w_xq)
                            tensor.wait_ge(self.s_woev.h, 32)
                        tensor.wait_ge(self.s_wgld.h, w_wg)
                        if unit >= 2:
                            tensor.wait_ge(self.s_sil.h, unit - 1)
                            tensor.wait_ge(self.s_hmul.h, unit - 1)
                        for a in range(8):
                            tensor.matmul(self.pbank[cbuf][:, 0:fb],
                                          self.xqfm[:, a, 128 * c:128 * (c + 1)],
                                          self.wgld[:, 0, wbuf, a, 0:fb],
                                          start=(a == 0), stop=(a == 7))
                        for a in range(8):
                            inst = tensor.matmul(self.pbank[2 + cbuf][:, 0:fb],
                                                 self.xqfm[:, a, 128 * c:128 * (c + 1)],
                                                 self.wgld[:, 1, wbuf, a, 0:fb],
                                                 start=(a == 0), stop=(a == 7))
                        inst.then_inc(self.s_gmm.h, 1)

                    w_sil = self.s_sil.bump(1)

                    @prog.on("scalar")
                    def _(scalar, cbuf=cbuf, fb=fb, w=w_mm, unit=unit):
                        scalar.wait_ge(self.s_gmm.h, w)
                        if unit >= 1:
                            scalar.wait_ge(self.s_hmul.h, unit)
                        scalar.activation(self.sgate[:, 0:fb], self.pbank[cbuf][:, 0:fb], AF.Silu)
                        scalar.drain().then_inc(self.s_sil.h, 1)

                    w_hm = self.s_hmul.bump(1)

                    @prog.on("vector")
                    def _(vector, cbuf=cbuf, fb=fb, w=w_sil, unit=unit):
                        vector.wait_ge(self.s_sil.h, w)
                        vector.tensor_tensor(self.h32[:, cbuf, 0:fb], self.sgate[:, 0:fb],
                                             self.pbank[2 + cbuf][:, 0:fb], ALU.mult)
                        vector.drain().then_inc(self.s_hmul.h, 1)

                    dwaits = [(self.s_hqst, 16 * (unit - 1))] if unit >= 2 else []
                    self.emit_quant(self.h32[:, cbuf, 0:fb], self.hqt2[:, cbuf, 0:fb],
                                    xwaits=[(self.s_hmul, w_hm)], dwaits=dwaits)
                    qa_after[unit] = self.s_a2d.n

                    w_q = self.s_d2a.bump(1)

                    @prog.on("vector")
                    def _(vector):
                        vector.drain().then_inc(self.s_d2a.h, 1)

                    self.dma("sync", self.s_hqst, self.hqb[128 * c:128 * (c + 1), i0:i0 + fb],
                             self.hqt2[:, cbuf, 0:fb], waits=[(self.s_d2a, w_q)])
                    unit += 1

            if self.upto == "c":
                self.emit_placeholder_out(extra_waits=[(self.s_hqst, self.s_hqst.n),
                                                       (self.s_woev, self.s_woev.n)])
            self.run_block()

    def emit_block_d(self):
        nc = self.nc
        prog = self.prog
        with contextlib.ExitStack() as stack:
            ec = stack.enter_context
            sb = lambda name, shape, dt=F32: ec(nc.sbuf_tensor(name, shape, dt))
            self.hqfm = sb("hqfm", [128, 22, 2048], BF16)
            self.wdq = sb("wdq", [128, 8, 1024], BF16)

            for ic in range(22):
                self.dma("sync", self.s_hqld, self.hqfm[:, ic, :],
                         self.hqb[:, 128 * ic:128 * (ic + 1)], transpose=True,
                         waits=[(self.s_hqst, self.s_hqst.n)] if ic == 0 else [])
            w_hq = self.s_hqld.n

            du = 0
            for p, panel in enumerate(D_PANELS):
                for j, ic in enumerate(panel):
                    waits = []
                    if p == 0 and j == 0:
                        waits.append((self.s_cc, 3))
                    if p >= 1 and j == 0:
                        waits.append((self.s_dmm, 16 * p))
                    self.dma("sync", self.s_wdld, self.wdq[:, j, :],
                             self.dq_out[:, 128 * ic:128 * (ic + 1)],
                             waits=waits, transpose=True)
                w_wd = self.s_wdld.n

                for c in range(NCHUNK):
                    w_mm = self.s_dmm.bump(1)

                    @prog.on("tensor")
                    def _(tensor, c=c, np_=len(panel), w_wd=w_wd, w_hq=w_hq, du=du,
                          first=(du == 0)):
                        if first:
                            tensor.wait_ge(self.s_hqld.h, w_hq)
                        tensor.wait_ge(self.s_wdld.h, w_wd)
                        if du >= 2:
                            tensor.wait_ge(self.s_dev.h, du - 1)
                        b0 = self.pbank[(c % 2) * 2]
                        b1 = self.pbank[(c % 2) * 2 + 1]
                        for j in range(np_):
                            tensor.matmul(b0[:], self.hqfm[:, self.panel_base[du // 16] + j, 128 * c:128 * (c + 1)],
                                          self.wdq[:, j, 0:512], start=(j == 0), stop=(j == np_ - 1))
                        for j in range(np_):
                            inst = tensor.matmul(b1[:], self.hqfm[:, self.panel_base[du // 16] + j, 128 * c:128 * (c + 1)],
                                                 self.wdq[:, j, 512:1024], start=(j == 0), stop=(j == np_ - 1))
                        inst.then_inc(self.s_dmm.h, 1)

                    w_ev = self.s_dev.bump(1)

                    @prog.on("vector")
                    def _(vector, c=c, w=w_mm):
                        vector.wait_ge(self.s_dmm.h, w)
                        b0 = self.pbank[(c % 2) * 2]
                        b1 = self.pbank[(c % 2) * 2 + 1]
                        vector.tensor_tensor(self.acc[:, 1024 * c:1024 * c + 512],
                                             self.acc[:, 1024 * c:1024 * c + 512], b0[:], ALU.add)
                        vector.tensor_tensor(self.acc[:, 1024 * c + 512:1024 * (c + 1)],
                                             self.acc[:, 1024 * c + 512:1024 * (c + 1)], b1[:], ALU.add)
                        vector.drain().then_inc(self.s_dev.h, 1)
                    du += 1

            for c in range(NCHUNK):
                self.dma("sync", self.s_out, self.out_dram[128 * c:128 * (c + 1), :],
                         self.acc[:, 1024 * c:1024 * (c + 1)],
                         waits=[(self.s_dev, 32 + c + 1)])

            @prog.on("gpsimd")
            def _(gpsimd):
                gpsimd.wait_ge(self.s_out.h, self.s_out.n)
            self.run_block()

    def emit_placeholder_out_wq(self):
        prog = self.prog
        if "wshq_dbg" in self.debug:
            wshq_dbg = self.nc.dram_tensor("wshq_dbg", [128, 1024], BF16, kind="ExternalOutput").ap()

            @prog.on("gpsimd")
            def _(gpsimd):
                gpsimd.wait_ge(self.s_wqst.h, self.s_wqst.n)
                gpsimd.wait_ge(self.s_cc.h, self.s_cc.n)
                gpsimd.dma_start(out=wshq_dbg, in_=self.wshq[:]).then_inc(self.s_stld.h, 16)
            self.s_stld.bump(16)
        for i in range(NCHUNK):
            self.dma("gpsimd", self.s_out, self.out_dram[128 * i:128 * (i + 1), :],
                     self.x_in[128 * i:128 * (i + 1), :])

        @prog.on("gpsimd")
        def _(gpsimd):
            gpsimd.wait_ge(self.s_out.h, self.s_out.n)
            gpsimd.wait_ge(self.s_wqst.h, self.s_wqst.n)
            gpsimd.wait_ge(self.s_cc.h, self.s_cc.n)
            gpsimd.wait_ge(self.s_stld.h, self.s_stld.n)

    def emit_placeholder_out(self, extra_waits=()):
        prog = self.prog
        for i in range(NCHUNK):
            self.dma("gpsimd", self.s_out, self.out_dram[128 * i:128 * (i + 1), :],
                     self.x_in[128 * i:128 * (i + 1), :])

        @prog.on("gpsimd")
        def _(gpsimd, extra=tuple(extra_waits)):
            gpsimd.wait_ge(self.s_out.h, self.s_out.n)
            gpsimd.wait_ge(self.s_xqst.h, self.s_xqst.n)
            gpsimd.wait_ge(self.s_pst.h, self.s_pst.n)
            gpsimd.wait_ge(self.s_vst.h, self.s_vst.n)
            gpsimd.wait_ge(self.s_pref.h, self.s_pref.n)
            gpsimd.wait_ge(self.s_wqst.h, self.s_wqst.n)
            gpsimd.wait_ge(self.s_cc.h, self.s_cc.n)
            for s, v in extra:
                gpsimd.wait_ge(s.h, v)

    def emit_all(self):
        self.panel_base = [0, 8, 16]
        stack = contextlib.ExitStack()
        with stack:
            self.ec = stack.enter_context
            self.declare_global()
            self.emit_block_a()
            if self.upto in ("b", "c", "full"):
                self.emit_block_b()
            if self.upto in ("c", "full"):
                self.emit_block_c()
            if self.upto == "full":
                self.emit_block_d()
        return self.nc


def make_consts():
    c = np.zeros((128, len(CONSTS)), np.float32)
    for k, v in CONSTS.items():
        c[:, CONST_IDX[k]] = v
    return c


def kernel(L0_attn_x, L0_attn_W_q, L0_attn_W_k, L0_attn_W_v, L0_attn_W_o,
           L0_ffn_w_gate, L0_ffn_w_up, L0_ffn_w_down, debug=(), upto="full"):
    x = np.ascontiguousarray(np.asarray(L0_attn_x, np.float32).reshape(16384, D))
    consts = make_consts()
    wqT = np.ascontiguousarray(np.asarray(L0_attn_W_q, np.float32).T)
    wkT = np.ascontiguousarray(np.asarray(L0_attn_W_k, np.float32).T)
    wvT = np.ascontiguousarray(np.asarray(L0_attn_W_v, np.float32).T)
    woT = np.ascontiguousarray(np.asarray(L0_attn_W_o, np.float32).T)
    wg = np.asarray(L0_ffn_w_gate, np.float32)
    wu = np.asarray(L0_ffn_w_up, np.float32)
    wd = np.asarray(L0_ffn_w_down, np.float32)
    mask = np.triu(np.ones((128, 128), np.float32)).astype(BF)

    b = Build(debug=debug, upto=upto)
    nc = b.emit_all()
    in_maps = []
    for c in range(NCORES):
        prefw = np.zeros((128, NCORES), np.float32)
        for r in range(4 * (c // 4), c):
            prefw[:, r] = 1.0
        in_maps.append({
            "x": np.ascontiguousarray(x[TOK * c:TOK * (c + 1)]),
            "consts": consts,
            "wqT": wqT, "wkT": wkT, "wvT": wvT, "woT": woT,
            "mask": np.ascontiguousarray(mask),
            "prefw": prefw,
            "wg_sh": np.ascontiguousarray(wg[GROW * c:GROW * (c + 1)]),
            "wu_sh": np.ascontiguousarray(wu[GROW * c:GROW * (c + 1)]),
            "wd_sh": np.ascontiguousarray(wd[DROW * c:DROW * (c + 1)]),
        })
    res = run_bass_kernel_spmd(nc, in_maps, core_ids=list(range(NCORES)))
    out = np.concatenate([r["out"] for r in res.results], axis=0)
    if debug:
        return out.reshape(2, 8192, D), res
    return out.reshape(2, 8192, D)
